# version 07
# baseline (speedup 1.0000x reference)
import numpy as np
import concourse.bass as bass
import concourse.mybir as mybir
from concourse.bass_utils import run_bass_kernel_spmd
from contextlib import ExitStack

F32 = mybir.dt.float32
BF16 = mybir.dt.bfloat16
AF = mybir.ActivationFunctionType
OP = mybir.AluOpType

P = 128
KC = 8
HC = 32
NH = 16
D = 64
TOK = 2048
TN = 512
NT = TOK // TN
HN = 256
EPS = 1e-5
NCORE = 8
PER_B = 4


class PB:
    def __init__(self):
        self.plan = []
        self.cnt = {}

    def emit(self, eng, fn, inc=None):
        self.plan.append((eng, fn, inc))
        if inc is not None:
            self.cnt[inc[0]] = self.cnt.get(inc[0], 0) + inc[1]

    def wait(self, eng, sem, val=None):
        v = self.cnt.get(sem, 0) if val is None else val
        if v <= 0:
            return
        self.plan.append((eng, ("w", sem, v), None))

    def snap(self, sem):
        return self.cnt.get(sem, 0)

    def replay(self, eng_name, e, semh):
        for eng, fn, inc in self.plan:
            if eng != eng_name:
                continue
            if isinstance(fn, tuple):
                e.wait_ge(semh[fn[1]], fn[2])
            else:
                inst = fn(e)
                if inc is not None and inst is not None:
                    inst.then_inc(semh[inc[0]], inc[1])


def build(dbg=False):
    nc = bass.Bass("TRN2", target_bir_lowering=False, detect_race_conditions=False)
    pb = PB()

    xTd = nc.dram_tensor("xT", [KC, P, TOK], F32, kind="ExternalInput")
    xlod = nc.dram_tensor("xlora", [NH, 1024], F32, kind="ExternalInput")
    wqtd = nc.dram_tensor("wq_t", [KC, P, 1024], F32, kind="ExternalInput")
    wvtd = nc.dram_tensor("wv_t", [KC, P, 1024], F32, kind="ExternalInput")
    wkd = nc.dram_tensor("wk", [KC, P, 1024], F32, kind="ExternalInput")
    wod = nc.dram_tensor("wo", [KC, P, 1024], F32, kind="ExternalInput")
    w1d = nc.dram_tensor("w1_t", [KC, P, 4096], F32, kind="ExternalInput")
    w2d = nc.dram_tensor("w2_t", [HC, P, 1024], F32, kind="ExternalInput")
    wbd = nc.dram_tensor("wbeta", [KC, P, NH], F32, kind="ExternalInput")
    bseld = nc.dram_tensor("bsel", [KC, P, NH], F32, kind="ExternalInput")
    bselTd = nc.dram_tensor("bselT", [NH, 1024], F32, kind="ExternalInput")
    bcold = nc.dram_tensor("bcol_d", [NH, 1], F32, kind="ExternalInput")
    bocold = nc.dram_tensor("bocol_d", [P, KC], F32, kind="ExternalInput")
    lngd = nc.dram_tensor("lngc_d", [P, KC], F32, kind="ExternalInput")
    lnbd = nc.dram_tensor("lnbc_d", [P, KC], F32, kind="ExternalInput")
    b1cd = nc.dram_tensor("b1c_d", [P, HC], F32, kind="ExternalInput")
    b2cd = nc.dram_tensor("b2c_d", [P, KC], F32, kind="ExternalInput")
    sccd = nc.dram_tensor("sccol_d", [P, 1], F32, kind="ExternalInput")
    idind = nc.dram_tensor("idin", [P, P], F32, kind="ExternalInput")
    seld = nc.dram_tensor("selS_d", [PER_B, PER_B], F32, kind="ExternalInput")
    pmkd = nc.dram_tensor("pmk_d", [PER_B, 1], F32, kind="ExternalInput")
    outD = nc.dram_tensor("outD", [KC, P, TOK], F32, kind="ExternalOutput")
    xn_scr = nc.dram_tensor("xn_scr", [P, KC, TOK], BF16)
    o_scr = nc.dram_tensor("o_scr", [P, KC, TOK], BF16)
    g_scr = nc.dram_tensor("g_scr", [P, KC, TOK], F32)
    w1_bscr = nc.dram_tensor("w1_bscr", [KC, P, 4096], BF16)
    w2_bscr = nc.dram_tensor("w2_bscr", [HC, P, 1024], BF16)
    agin = nc.dram_tensor("agin", [1, 1152], F32)
    agout = nc.dram_tensor("agout", [PER_B, 1152], F32)
    if dbg:
        xnD = nc.dram_tensor("xnD", [P, KC, TOK], BF16, kind="ExternalOutput")
        oD = nc.dram_tensor("oD", [P, KC, TOK], BF16, kind="ExternalOutput")
        gD = nc.dram_tensor("gD", [P, KC, TOK], F32, kind="ExternalOutput")
        agD = nc.dram_tensor("agD", [PER_B, 1152], F32, kind="ExternalOutput")
        hD = nc.dram_tensor("hD", [P, HC, HN], BF16, kind="ExternalOutput")
        tD = nc.dram_tensor("tD", [NH, HN], BF16, kind="ExternalOutput")
        t2wD = nc.dram_tensor("t2wD", [P, HC, NH], BF16, kind="ExternalOutput")
        e1zD = nc.dram_tensor("e1zD", [4, HC, P], BF16, kind="ExternalOutput")
        l2zD = nc.dram_tensor("l2zD", [NH, KC, P], BF16, kind="ExternalOutput")
        xnlD = nc.dram_tensor("xnlD", [P, KC, NH], BF16, kind="ExternalOutput")
        xlrD = nc.dram_tensor("xlrD", [NH, 1024], BF16, kind="ExternalOutput")
        colD = nc.dram_tensor("colD", [NH, 4], F32, kind="ExternalOutput")

    with ExitStack() as es0:
        A = lambda name, shp, dt: es0.enter_context(nc.sbuf_tensor(name, shp, dt))
        qvbw = A("qvbw", [P, KC, 100], BF16)
        xlr = A("xlr", [NH, 1024], BF16)
        xlrF = A("xlrF", [NH, 1024], F32)
        xnlT = A("xnlT", [P, KC, NH], BF16)
        e1z = A("e1z", [4, HC, P], BF16)
        l2z = A("l2z", [NH, KC, P], BF16)
        t2w = A("t2w", [P, HC, NH], BF16)
        bselB = A("bselB", [P, KC, NH], BF16)
        bselTB = A("bselTB", [NH, 1024], BF16)
        bcol = A("bcol", [NH, 1], F32)
        bocol = A("bocol", [P, KC], F32)
        lngc = A("lngc", [P, KC], F32)
        lnbc = A("lnbc", [P, KC], F32)
        b1c = A("b1c", [P, HC], F32)
        b2c = A("b2c", [P, KC], F32)
        sccol = A("sccol", [P, 1], F32)
        epsc = A("epsc", [P, 1], F32)
        idinS = A("idinS", [P, P], F32)
        selS = A("selS", [PER_B, PER_B], F32)
        pmk = A("pmk", [PER_B, 1], F32)
        onesb = A("onesb", [P, P], BF16)
        xloJ = A("xloJ", [NH, 1024], F32)
        colA = A("colA", [NH, 1], F32)
        colB = A("colB", [NH, 1], F32)
        colC = A("colC", [NH, 1], F32)
        mrk = A("mrk", [1, 4], F32)
        zc = A("zc", [NH, 8], F32)
        t1r = A("t1r", [4, TOK], BF16)
        q2b = A("q2b", [NH, TOK], BF16)
        hstate = A("hstate", [P, 9], F32)
        urow9 = A("urow9", [9, P], F32)
        agS = A("agS", [PER_B, 1152], F32)
        wmat = A("wmat", [PER_B, NH], F32)
        wmat2 = A("wmat2", [PER_B, NH], F32)
        wm2tb = A("wm2tb", [NH, PER_B], BF16)
        hgS = A("hgS", [P, KC], F32)
        hbd = A("hbd", [P, KC, NH], BF16)
        hbdlo = A("hbdlo", [P, KC, NH], BF16)
        whb = A("whb", [NH, 1024], BF16)
        whblo = A("whblo", [NH, 1024], BF16)
        q2lo = A("q2lo", [NH, TOK], BF16)
        wobf = A("wobf", [P, KC, 1024], BF16)
        xnb = A("xnb", [P, KC, TN], BF16)
        ob = A("ob", [P, KC, TN], BF16)
        gt = A("gt", [P, KC, TN], F32)
        muS = A("muS", [P, TN], F32)
        varS = A("varS", [P, TN], F32)
        rstdS = A("rstdS", [P, TN], F32)
        tmpA = A("tmpA", [P, TN], F32)
        tmpC = A("tmpC", [P, TN], F32)
        pbank = [es0.enter_context(nc.psum_tensor("pb%d" % i, [P, 512], F32))
                 for i in range(8)]
        psA, psB, psC, psD, psE, psF, psG, psm = pbank

        semh = {}
        for s in ("dI", "dS", "dO", "eT", "eV", "eS", "cc"):
            semh[s] = es0.enter_context(nc.semaphore(s))

        def dma(eng, sem, dst, src):
            pb.emit(eng, lambda e, d=dst, s=src: e.dma_start(out=d, in_=s), (sem, 16))

        def mm(out, lhsT, rhs, start, stop, inc=None):
            pb.emit("tensor",
                    lambda e, o=out, l=lhsT, r=rhs, st=start, sp=stop:
                    e.matmul(o, l, r, start=st, stop=sp), inc)

        def vop(fn, inc=None):
            pb.emit("vector", fn, inc)

        def sop(fn, inc=None):
            pb.emit("scalar", fn, inc)

        def gop(fn, inc=None):
            pb.emit("gpsimd", fn, inc)

        def vmark():
            vop(lambda e: e.memset(mrk[0:1, 0:4], 0.0), ("eV", 1))

        for dst, src in [(bcol[:], bcold[:, :]), (bocol[:], bocold[:, :]),
                         (lngc[:], lngd[:, :]), (lnbc[:], lnbd[:, :]),
                         (b1c[:], b1cd[:, :]), (b2c[:], b2cd[:, :]),
                         (sccol[:], sccd[:, :]), (idinS[:], idind[:, :]),
                         (selS[:], seld[:, :]), (pmk[:], pmkd[:, :]),
                         (xloJ[:], xlod[:, :])]:
            dma("sync", "dI", dst, src)

        gop(lambda e: e.memset(onesb[:, :], 1.0))
        gop(lambda e: e.memset(epsc[:, :], EPS))
        gop(lambda e: e.memset(hstate[:, :], 0.0))
        gop(lambda e: e.memset(qvbw[:, :, :], 0.0))
        gop(lambda e: e.memset(e1z[0:4, :, :], 0.0))
        gop(lambda e: e.memset(l2z[:, :, :], 0.0))
        gop(lambda e: e.memset(t2w[:, :, :], 0.0), ("cc", 1))
        pb.wait("gpsimd", "cc", 1)
        for kc in range(KC):
            dma("gpsimd", "dS", bselB[:, kc, :], bseld[kc, :, :])
            dma("gpsimd", "dS", qvbw[:, kc, 64:80], wbd[kc, :, :])
        dma("gpsimd", "dS", bselTB[:], bselTd[:, :])
        smallcast_mark = pb.snap("dS")

        with ExitStack() as esA:
            AA = lambda name, shp, dt: esA.enter_context(nc.sbuf_tensor(name, shp, dt))
            wkbf = AA("wkbf", [P, KC, 1024], BF16)
            xt32 = AA("xt32", [P, KC, TN], F32)
            xb = AA("xb", [P, KC, TN], BF16)
            x2b = AA("x2b", [P, KC, TN], BF16)
            vsbS = AA("vsbS", [P, TN], F32)
            d0s = AA("d0s", [P, TN], F32)
            d1s = AA("d1s", [P, TN], F32)
            hsc = AA("hsc", [P, TN], F32)
            betar = AA("betar", [NH, TOK], F32)
            qsr = AA("qsr", [NH, TOK], F32)
            cpb = AA("cpb", [NH, TOK], F32)
            betab16 = AA("betab16", [NH, TOK], BF16)
            betalo = AA("betalo", [NH, TOK], BF16)
            qsrb = AA("qsrb", [NH, TOK], BF16)
            vsrb = AA("vsrb", [NH, TOK], BF16)

            gop(lambda e: e.memset(zc[:, :], 0.0), ("cc", 1))

            pb.wait("vector", "dI")
            vop(lambda e: e.tensor_reduce(colA[:, 0:1], xloJ[:, :], mybir.AxisListType.X, OP.add), ("eV", 1))
            pb.wait("vector", "eV")
            vop(lambda e: e.tensor_scalar(colC[:, 0:1], colA[:, 0:1], 1.0 / 1024, None, OP.mult), ("eV", 1))
            pb.wait("vector", "eV")
            vop(lambda e: e.tensor_scalar(xlrF[:, :], xloJ[:, :], colC[:, 0:1], None, OP.subtract), ("eV", 1))
            pb.wait("scalar", "eV")
            sop(lambda e: e.activation(xloJ[:, :], xlrF[:, :], AF.Square, accum_out=colB[:, 0:1]))
            sop(lambda e: e.activation(colC[:, 0:1], colB[:, 0:1], AF.Copy), ("eS", 1))
            pb.wait("vector", "eS")
            vop(lambda e: e.tensor_scalar(colB[:, 0:1], colC[:, 0:1], 1.0 / 1024, None, OP.mult), ("eV", 1))
            pb.wait("scalar", "eV")
            sop(lambda e: e.activation(colA[:, 0:1], colB[:, 0:1], AF.Ln, bias=epsc[0:NH, 0:1]))
            sop(lambda e: e.activation(colB[:, 0:1], colA[:, 0:1], AF.Exp, scale=-0.5), ("eS", 1))
            pb.wait("vector", "eS")
            vop(lambda e: e.tensor_scalar(xloJ[:, :], xlrF[:, :], colB[:, 0:1], None, OP.mult))
            vop(lambda e: e.tensor_copy(xlr[:, :], xloJ[:, :]), ("eV", 1))

            pb.wait("tensor", "dS", smallcast_mark)
            pb.wait("tensor", "dI")
            for kc in range(KC):
                pb.wait("tensor", "eV")
                mm(psm[0:P, 0:NH], xloJ[0:NH, kc * P:(kc + 1) * P], idinS[0:NH, 0:NH],
                   True, True, inc=("eT", 1))
                pb.wait("vector", "eT")
                vop(lambda e, kc=kc: e.tensor_copy(xnlT[:, kc, :], psm[0:P, 0:NH]), ("eV", 1))

            pb.wait("vector", "cc", 2)
            for kc in range(KC):
                vop(lambda e, kc=kc: e.tensor_copy(qvbw[:, kc, 96:100], xnlT[:, kc, 1:9:2]))
            for hc in range(HC):
                m = hc // 8
                sub = hc % 8
                for i in range(4):
                    vop(lambda e, hc=hc, sub=sub, i=i, m=m: e.tensor_copy(
                        t2w[:, hc, 4 * i + m:4 * i + m + 1], xnlT[:, sub, 9 + 2 * i:10 + 2 * i]))
            vmark()
            pb.wait("gpsimd", "eV")
            for i in range(4):
                for sub in range(KC):
                    hc = 8 * i + sub
                    dma("gpsimd", "dS", e1z[i:i + 1, hc, :],
                        xlr[2 * i:2 * i + 1, sub * P:(sub + 1) * P])
            for oc in range(KC):
                i = oc // 2
                p0 = (oc % 2) * P
                for m in range(4):
                    st = 4 * p0 + m
                    pb.emit("gpsimd", lambda e, i=i, m=m, oc=oc, st=st: e.dma_start(
                        out=l2z[4 * i + m:4 * i + m + 1, oc, :],
                        in_=xlr[8 + 2 * i:9 + 2 * i, st:st + 4 * (P - 1) + 1:4],
                        allow_slow_non_contiguous=True), ("dS", 16))

            for kc in range(KC):
                dma("gpsimd", "dS", wkbf[:, kc, :], wqtd[kc, :, :])
            wq_mark = pb.snap("dS")

            def wqs_mm(dst_off, ready_mark):
                pb.wait("tensor", "dS", ready_mark)
                for dc in range(KC):
                    pb.wait("tensor", "eV")
                    for jc in range(KC):
                        mm(psm[0:P, 16:32], wkbf[:, jc, dc * P:(dc + 1) * P],
                           bselB[:, jc, :], jc == 0, jc == KC - 1,
                           inc=("eT", 1) if jc == KC - 1 else None)
                    pb.wait("vector", "eT")
                    vop(lambda e, dc=dc, o=dst_off: e.tensor_copy(
                        qvbw[:, dc, o:o + 16], psm[0:P, 16:32]), ("eV", 1))

            wqs_mm(0, wq_mark)
            pb.wait("gpsimd", "eV")
            for kc in range(KC):
                dma("gpsimd", "dS", wkbf[:, kc, :], wvtd[kc, :, :])
            wv_mark = pb.snap("dS")
            wqs_mm(32, wv_mark)
            pb.wait("gpsimd", "eV")
            for kc in range(KC):
                dma("gpsimd", "dS", wkbf[:, kc, :], wkd[kc, :, :])
            for kc in range(KC):
                dma("gpsimd", "dS", wobf[:, kc, :], wod[kc, :, :])
            wk_mark = pb.snap("dS")
            for jc in range(KC):
                dma("gpsimd", "dS", w1_bscr[jc, :, :], w1d[jc, :, :])
            w1cast_mark = pb.snap("dS")

            w2cast_marks = []
            for tt in range(NT):
                toks = slice(tt * TN, (tt + 1) * TN)
                pb.wait("sync", "eV")
                pb.wait("sync", "eS")
                for kc in range(KC):
                    dma("sync", "dI", xt32[:, kc, :], xTd[kc, :, toks])
                pb.wait("vector", "dI")
                pb.wait("scalar", "dI")
                pb.wait("scalar", "eT")
                for kc in range(KC):
                    vop(lambda e, kc=kc: e.tensor_copy(xb[:, kc, :], xt32[:, kc, :]),
                        ("eV", 1) if kc == KC - 1 else None)
                for kc in range(KC):
                    sop(lambda e, kc=kc: e.activation(x2b[:, kc, :], xt32[:, kc, :], AF.Square),
                        ("eS", 1) if kc == KC - 1 else None)
                pb.wait("tensor", "eV")
                pb.wait("tensor", "eS")
                for kc in range(KC):
                    mm(psC[:, :], onesb[:, :], xb[:, kc, :], kc == 0, kc == KC - 1,
                       inc=("eT", 1) if kc == KC - 1 else None)
                for kc in range(KC):
                    mm(psD[:, :], onesb[:, :], x2b[:, kc, :], kc == 0, kc == KC - 1,
                       inc=("eT", 1) if kc == KC - 1 else None)
                pb.wait("vector", "eT")
                vop(lambda e: e.tensor_scalar(muS[:, :], psC[:, :], 1.0 / 1024, None, OP.mult), ("eV", 1))
                pb.wait("scalar", "eV")
                sop(lambda e: e.activation(tmpA[:, :], muS[:, :], AF.Square), ("eS", 1))
                pb.wait("vector", "eS")
                vop(lambda e: e.scalar_tensor_tensor(varS[:, :], psD[:, :], 1.0 / 1024,
                                                     tmpA[:, :], OP.mult, OP.subtract), ("eV", 1))
                pb.wait("scalar", "eV")
                sop(lambda e: e.activation(tmpA[:, :], varS[:, :], AF.Ln, bias=epsc[:, 0:1]))
                sop(lambda e: e.activation(rstdS[:, :], tmpA[:, :], AF.Exp, scale=-0.5), ("eS", 1))
                pb.wait("vector", "eS")
                pb.wait("vector", "dS")
                for kc in range(KC):
                    vop(lambda e, kc=kc: e.tensor_tensor(tmpC[:, :], xt32[:, kc, :], muS[:, :], OP.subtract))
                    vop(lambda e, kc=kc: e.tensor_tensor(xnb[:, kc, :], tmpC[:, :], rstdS[:, :], OP.mult))
                vmark()
                pb.wait("gpsimd", "eV")
                dma("gpsimd", "dS", xn_scr[:, :, toks], xnb[:, :, :])

                pb.wait("tensor", "eV")
                pb.wait("tensor", "dS", wk_mark)
                for jc in range(KC):
                    mm(psm[0:100, :], qvbw[:, jc, :], xnb[:, jc, :], jc == 0, jc == KC - 1,
                       inc=("eT", 1) if jc == KC - 1 else None)
                pb.wait("scalar", "eT")
                pb.wait("scalar", "eV")
                sop(lambda e, toks=toks: e.activation(betar[:, toks], psm[64:80, :], AF.Sigmoid, bias=bcol[:, 0:1]))
                sop(lambda e, toks=toks: e.activation(betab16[:, toks], betar[:, toks], AF.Copy))
                sop(lambda e, toks=toks: e.activation(qsrb[:, toks], psm[0:16, :], AF.Copy))
                sop(lambda e, toks=toks: e.activation(qsr[:, toks], psm[0:16, :], AF.Copy))
                sop(lambda e, toks=toks: e.activation(vsrb[:, toks], psm[32:48, :], AF.Copy))
                sop(lambda e, toks=toks: e.activation(t1r[:, toks], psm[96:100, :], AF.Copy,
                                                      scale=sccol[0:4, 0:1]), ("eS", 1))
                pb.wait("vector", "eS")
                vop(lambda e, toks=toks: e.tensor_tensor(betalo[:, toks], betar[:, toks],
                                                         betab16[:, toks], OP.subtract), ("eV", 1))

                pb.wait("vector", "dS")
                for fc in range(KC):
                    psk = pbank[fc % 2]
                    pb.wait("tensor", "eV")
                    pb.wait("tensor", "eS")
                    for jc in range(KC):
                        mm(psk[:, :], wkbf[:, jc, fc * P:(fc + 1) * P], xnb[:, jc, :],
                           jc == 0, jc == KC - 1)
                    bsl = bselTB[0:NH, fc * P:(fc + 1) * P]
                    mm(psE[:, :], bsl, vsrb[:, toks], True, True)
                    mm(psF[:, :], bsl, qsrb[:, toks], True, True)
                    mm(psG[:, :], bsl, betab16[:, toks], True, False)
                    mm(psG[:, :], bsl, betalo[:, toks], False, True, inc=("eT", 1))
                    pb.wait("scalar", "eT")
                    pb.wait("scalar", "eV")
                    sop(lambda e: e.activation(vsbS[:, :], psE[:, :], AF.Copy))
                    sop(lambda e: e.activation(d0s[:, :], psG[:, :], AF.Copy), ("eS", 1))
                    pb.wait("vector", "eT")
                    pb.wait("vector", "eS")
                    vop(lambda e, psk=psk: e.tensor_tensor(d1s[:, :], psk[:, :], vsbS[:, :], OP.mult))
                    vop(lambda e, fc=fc: e.tensor_tensor_scan(
                        hsc[:, :], d0s[:, :], d1s[:, :], hstate[:, fc:fc + 1], OP.mult, OP.add), ("eV", 1))
                    pb.wait("vector", "eV")
                    vop(lambda e, fc=fc: e.tensor_copy(hstate[:, fc:fc + 1], hsc[:, TN - 1:TN]))
                    vop(lambda e, fc=fc: e.tensor_tensor(ob[:, fc, :], hsc[:, :], psF[:, :], OP.mult), ("eV", 1))
                pb.wait("gpsimd", "eV")
                dma("gpsimd", "dS", o_scr[:, :, toks], ob[:, :, :])
                for hcw in range(tt * 8, tt * 8 + 8):
                    dma("gpsimd", "dS", w2_bscr[hcw, :, :], w2d[hcw, :, :])
                w2cast_marks.append(pb.snap("dS"))

            sop(lambda e: e.activation(cpb[:, :], betar[:, :], AF.Ln, accum_out=colA[:, 0:1]))
            sop(lambda e: e.activation(colC[:, 0:1], colA[:, 0:1], AF.Copy), ("eS", 1))
            pb.wait("vector", "eS")
            vop(lambda e: e.tensor_copy(hstate[0:NH, 8:9], colC[:, 0:1]))
            vop(lambda e: e.tensor_tensor_scan(cpb[:, :], betar[:, :],
                                               zc[:, 0:1].broadcast_to([NH, TOK]),
                                               1.0, OP.mult, OP.add), ("eV", 1))
            pb.wait("vector", "eV")
            vop(lambda e: e.tensor_tensor(cpb[:, :], qsr[:, :], cpb[:, :], OP.mult), ("eV", 1))
            pb.wait("vector", "eV")
            vop(lambda e: e.tensor_copy(q2b[:, :], cpb[:, :]), ("eV", 1))
            pb.wait("vector", "eV")
            vop(lambda e: e.tensor_tensor(q2lo[:, :], cpb[:, :], q2b[:, :], OP.subtract), ("eV", 1))
            pb.wait("tensor", "eV")
            mm(psm[0:9, 0:P], hstate[:, 0:9], idinS[:, :], True, True, inc=("eT", 1))
            pb.wait("vector", "eT")
            vop(lambda e: e.tensor_copy(urow9[:, :], psm[0:9, 0:P]), ("eV", 1))
            pb.wait("gpsimd", "eV")
            dma("gpsimd", "dS", bass.AP(agin, 0, [[P, 9], [1, P]]), urow9[0:9, :])
            pb.wait("gpsimd", "dS")
            rg = [[0, 1, 2, 3], [4, 5, 6, 7]]
            pb.emit("gpsimd", lambda e: e.collective_compute(
                "AllGather", OP.bypass, replica_groups=rg,
                ins=[agin.ap().opt()], outs=[agout.ap().opt()]), ("cc", 1))
            pb.wait("gpsimd", "cc")
            dma("gpsimd", "dS", agS[:, :], agout[:, :])
            pb.wait("tensor", "dS")
            mm(psm[0:PER_B, 0:NH], selS[:, :], agS[0:PER_B, 1024:1040], True, True, inc=("eT", 1))
            pb.wait("scalar", "eT")
            sop(lambda e: e.activation(wmat[:, :], psm[0:PER_B, 0:NH], AF.Exp), ("eS", 1))
            pb.wait("vector", "eS")
            vop(lambda e: e.tensor_scalar(wmat2[:, :], wmat[:, :], pmk[:, 0:1], None, OP.mult), ("eV", 1))
            pb.wait("vector", "eV")
            pb.wait("tensor", "eV")
            mm(psm[0:NH, 32:32 + PER_B], wmat2[0:PER_B, 0:NH], idinS[0:PER_B, 0:PER_B],
               True, True, inc=("eT", 1))
            pb.wait("vector", "eT")
            vop(lambda e: e.tensor_copy(wm2tb[:, :], psm[0:NH, 32:32 + PER_B]), ("eV", 1))
            for fc in range(KC):
                pb.wait("tensor", "eV")
                mm(psm[0:P, 24:24 + PER_B], agS[0:PER_B, fc * P:(fc + 1) * P],
                   idinS[0:PER_B, 0:PER_B], True, True, inc=("eT", 1))
                mm(psD[:, 0:PER_B], bselTB[0:NH, fc * P:(fc + 1) * P], wm2tb[0:NH, 0:PER_B],
                   True, True, inc=("eT", 1))
                pb.wait("vector", "eT")
                vop(lambda e: e.tensor_copy(muS[:, 0:PER_B], psm[0:P, 24:24 + PER_B]), ("eV", 1))
                pb.wait("vector", "eV")
                vop(lambda e: e.tensor_tensor(varS[:, 0:PER_B], muS[:, 0:PER_B], psD[:, 0:PER_B], OP.mult), ("eV", 1))
                pb.wait("vector", "eV")
                vop(lambda e, fc=fc: e.tensor_reduce(hgS[:, fc:fc + 1], varS[:, 0:PER_B],
                                                     mybir.AxisListType.X, OP.add), ("eV", 1))
                pb.wait("vector", "eV")
            pb.wait("vector", "eV")
            for fc in range(KC):
                vop(lambda e, fc=fc: e.tensor_scalar(
                    varS[:, 0:NH], bselB[:, fc, :], hgS[:, fc:fc + 1], None, OP.mult), ("eV", 1))
                pb.wait("vector", "eV")
                vop(lambda e, fc=fc: e.tensor_copy(hbd[:, fc, :], varS[:, 0:NH]), ("eV", 1))
                pb.wait("vector", "eV")
                vop(lambda e, fc=fc: e.tensor_tensor(hbdlo[:, fc, :], varS[:, 0:NH],
                                                     hbd[:, fc, :], OP.subtract))
            vmark()
            for half in range(2):
                pb.wait("tensor", "eV")
                for fc in range(KC):
                    mm(psA[0:NH, :], hbd[:, fc, :], wobf[:, fc, half * 512:(half + 1) * 512],
                       fc == 0, False)
                for fc in range(KC):
                    mm(psA[0:NH, :], hbdlo[:, fc, :], wobf[:, fc, half * 512:(half + 1) * 512],
                       False, fc == KC - 1, inc=("eT", 1) if fc == KC - 1 else None)
                pb.wait("vector", "eT")
                vop(lambda e, half=half: e.tensor_copy(
                    whb[:, half * 512:(half + 1) * 512], psA[0:NH, :]), ("eV", 1))
                pb.wait("vector", "eV")
                vop(lambda e, half=half: e.tensor_tensor(
                    whblo[:, half * 512:(half + 1) * 512], psA[0:NH, :],
                    whb[:, half * 512:(half + 1) * 512], OP.subtract), ("eV", 1))

        with ExitStack() as esB:
            AB = lambda name, shp, dt: esB.enter_context(nc.sbuf_tensor(name, shp, dt))
            w1bf = AB("w1bf", [P, KC, 4096], BF16)
            hbt = AB("hbt", [P, HC, HN], BF16)
            w2s = AB("w2s", [P, KC, 1024], BF16)
            t2b = AB("t2b", [NH, HN], BF16)

            pb.wait("sync", "dS", w1cast_mark)
            pb.wait("sync", "eV")
            pb.wait("sync", "eS")
            pb.wait("sync", "eT")
            for jc in range(KC):
                dma("sync", "dI", w1bf[:, jc, :], w1_bscr[jc, :, :])
            w1_mark = pb.snap("dI")

            for tt in range(NT):
                toks = slice(tt * TN, (tt + 1) * TN)
                pb.wait("sync", "eT")
                for kc in range(KC):
                    dma("sync", "dI", xnb[:, kc, :], xn_scr[:, kc, toks])
                for hf in range(2):
                    hts = slice(tt * TN + hf * HN, tt * TN + (hf + 1) * HN)
                    lts = slice(hf * HN, (hf + 1) * HN)
                    pb.wait("tensor", "dI")
                    pb.wait("tensor", "eS")
                    if tt == 0 and hf == 0:
                        pb.wait("tensor", "dI", w1_mark)
                    gelu_marks = {}
                    for hc in range(HC):
                        ph = pbank[hc % 2]
                        if hc >= 2:
                            pb.wait("tensor", "eS", gelu_marks[hc - 2])
                        for jc in range(KC):
                            mm(ph[:, 0:HN], w1bf[:, jc, hc * P:(hc + 1) * P],
                               xnb[:, jc, lts], jc == 0, False)
                        mm(ph[:, 0:HN], e1z[0:4, hc, :], t1r[0:4, hts], False, True,
                           inc=("eT", 1))
                        pb.wait("scalar", "eT")
                        sop(lambda e, hc=hc, ph=ph, lts=lts: e.activation(
                            hbt[:, hc, 0:HN], ph[:, 0:HN], AF.Gelu, bias=b1c[:, hc:hc + 1]),
                            ("eS", 1))
                        gelu_marks[hc] = pb.snap("eS")
                        pb.wait("tensor", "eS", gelu_marks[hc])
                        mm(psC[0:NH, 0:HN], t2w[:, hc, :], hbt[:, hc, 0:HN],
                           hc == 0, hc == HC - 1,
                           inc=("eT", 1) if hc == HC - 1 else None)
                    pb.wait("vector", "eT")
                    vop(lambda e: e.tensor_scalar(t2b[:, :], psC[0:NH, 0:HN],
                                                  sccol[0:NH, 0:1], None, OP.mult), ("eV", 1))
                    pb.wait("tensor", "eV")
                    pb.wait("tensor", "eS")
                    grp_marks = {}
                    for grp in range(KC):
                        if grp < 2:
                            pb.wait("sync", "eT")
                        else:
                            pb.wait("sync", "eT", grp_marks[grp - 2])
                        if tt == 0 and hf == 0:
                            pb.wait("sync", "dS", w2cast_marks[min(grp // 2, NT - 1)])
                        for j in range(4):
                            hcr = grp * 4 + j
                            slot = hcr % KC
                            dma("sync", "dI", w2s[:, slot, :], w2_bscr[hcr, :, :])
                        pb.wait("tensor", "dI")
                        for j in range(4):
                            hcr = grp * 4 + j
                            slot = hcr % KC
                            for oc in range(KC):
                                mm(pbank[oc][:, 0:HN], w2s[:, slot, oc * P:(oc + 1) * P],
                                   hbt[:, hcr, 0:HN], hcr == 0, False,
                                   inc=("eT", 1) if (j == 3 and oc == KC - 1) else None)
                        grp_marks[grp] = pb.snap("eT")
                    pb.wait("tensor", "eV")
                    for oc in range(KC):
                        mm(pbank[oc][:, 0:HN], l2z[:, oc, :], t2b[0:NH, :], False, True,
                           inc=("eT", 1) if oc == KC - 1 else None)
                    pb.wait("scalar", "eT")
                    pb.wait("scalar", "dS")
                    for oc in range(KC):
                        sop(lambda e, oc=oc, lts=lts: e.activation(
                            gt[:, oc, lts], pbank[oc][:, 0:HN], AF.Silu, bias=b2c[:, oc:oc + 1]),
                            ("eS", 1) if oc == KC - 1 else None)
                pb.wait("gpsimd", "eS")
                if dbg and tt == NT - 1:
                    dma("gpsimd", "dS", hD[:, :, :], hbt[:, :, :])
                    dma("gpsimd", "dS", tD[:, :], t2b[:, :])
                    dma("gpsimd", "dS", t2wD[:, :, :], t2w[:, :, :])
                    dma("gpsimd", "dS", e1zD[0:4, :, :], e1z[0:4, :, :])
                    dma("gpsimd", "dS", l2zD[:, :, :], l2z[:, :, :])
                    dma("gpsimd", "dS", xnlD[:, :, :], xnlT[:, :, :])
                    dma("gpsimd", "dS", xlrD[:, :], xlr[:, :])
                    pb.emit("gpsimd", lambda e: e.dma_start(
                        out=bass.AP(colD, 0, [[4, NH], [1, 1]]), in_=colB[:, 0:1],
                        allow_slow_non_contiguous=True), ("dS", 16))
                    pb.emit("gpsimd", lambda e: e.dma_start(
                        out=bass.AP(colD, 1, [[4, NH], [1, 1]]), in_=colC[:, 0:1],
                        allow_slow_non_contiguous=True), ("dS", 16))
                    pb.emit("gpsimd", lambda e: e.dma_start(
                        out=bass.AP(colD, 2, [[4, NH], [1, 1]]), in_=colA[:, 0:1],
                        allow_slow_non_contiguous=True), ("dS", 16))
                dma("gpsimd", "dS", g_scr[:, :, toks], gt[:, :, :])

        with ExitStack() as esC:
            AC = lambda name, shp, dt: esC.enter_context(nc.sbuf_tensor(name, shp, dt))
            xt32c = AC("xt32c", [P, KC, TN], F32)
            yt = AC("yt", [P, KC, TN], F32)
            ybb = AC("ybb", [P, KC, TN], BF16)
            y2b = AC("y2b", [P, KC, TN], BF16)
            stl = AC("stl", [P, KC, TN], F32)

            for tt in range(NT):
                toks = slice(tt * TN, (tt + 1) * TN)
                pb.wait("gpsimd", "eT")
                pb.wait("gpsimd", "eV")
                pb.wait("gpsimd", "eS")
                dma("gpsimd", "dS", ob[:, :, :], o_scr[:, :, toks])
                dma("gpsimd", "dS", xnb[:, :, :], xn_scr[:, :, toks])
                dma("gpsimd", "dS", gt[:, :, :], g_scr[:, :, toks])
                pb.wait("sync", "eV")
                pb.wait("sync", "eS")
                pb.wait("sync", "eT")
                for kc in range(KC):
                    dma("sync", "dI", xt32c[:, kc, :], xTd[kc, :, toks])
                for oc in range(KC):
                    pa = pbank[oc % 2]
                    pb.wait("tensor", "eV")
                    pb.wait("tensor", "eS")
                    pb.wait("tensor", "dS")
                    for fc in range(KC):
                        mm(pa[:, :], wobf[:, fc, oc * P:(oc + 1) * P], ob[:, fc, :],
                           fc == 0, False)
                    mm(pa[:, :], whb[0:NH, oc * P:(oc + 1) * P], q2b[0:NH, toks],
                       False, False)
                    mm(pa[:, :], whblo[0:NH, oc * P:(oc + 1) * P], q2b[0:NH, toks],
                       False, False)
                    mm(pa[:, :], whb[0:NH, oc * P:(oc + 1) * P], q2lo[0:NH, toks],
                       False, True, inc=("eT", 1))
                    pb.wait("vector", "eT")
                    pb.wait("vector", "dS")
                    vop(lambda e, oc=oc, pa=pa: e.scalar_tensor_tensor(
                        yt[:, oc, :], pa[:, :], bocol[:, oc:oc + 1], xnb[:, oc, :],
                        OP.add, OP.add), ("eV", 1))
                pb.wait("vector", "eV")
                for oc in range(KC):
                    vop(lambda e, oc=oc: e.tensor_copy(ybb[:, oc, :], yt[:, oc, :]),
                        ("eV", 1) if oc == KC - 1 else None)
                pb.wait("scalar", "eV")
                for oc in range(KC):
                    sop(lambda e, oc=oc: e.activation(y2b[:, oc, :], yt[:, oc, :], AF.Square),
                        ("eS", 1) if oc == KC - 1 else None)
                pb.wait("tensor", "eV")
                pb.wait("tensor", "eS")
                for oc in range(KC):
                    mm(psC[:, :], onesb[:, :], ybb[:, oc, :], oc == 0, oc == KC - 1,
                       inc=("eT", 1) if oc == KC - 1 else None)
                for oc in range(KC):
                    mm(psD[:, :], onesb[:, :], y2b[:, oc, :], oc == 0, oc == KC - 1,
                       inc=("eT", 1) if oc == KC - 1 else None)
                pb.wait("vector", "eT")
                vop(lambda e: e.tensor_scalar(muS[:, :], psC[:, :], 1.0 / 1024, None, OP.mult), ("eV", 1))
                pb.wait("scalar", "eV")
                sop(lambda e: e.activation(tmpA[:, :], muS[:, :], AF.Square), ("eS", 1))
                pb.wait("vector", "eS")
                vop(lambda e: e.scalar_tensor_tensor(varS[:, :], psD[:, :], 1.0 / 1024,
                                                     tmpA[:, :], OP.mult, OP.subtract), ("eV", 1))
                pb.wait("scalar", "eV")
                sop(lambda e: e.activation(tmpA[:, :], varS[:, :], AF.Ln, bias=epsc[:, 0:1]))
                sop(lambda e: e.activation(rstdS[:, :], tmpA[:, :], AF.Exp, scale=-0.5), ("eS", 1))
                pb.wait("vector", "eS")
                pb.wait("vector", "dI")
                pb.wait("vector", "dO")
                for oc in range(KC):
                    vop(lambda e, oc=oc: e.tensor_tensor(tmpC[:, :], yt[:, oc, :], muS[:, :], OP.subtract))
                    vop(lambda e, oc=oc: e.tensor_tensor(tmpC[:, :], tmpC[:, :], rstdS[:, :], OP.mult), ("eV", 1))
                    pb.wait("vector", "eV")
                    vop(lambda e, oc=oc: e.tensor_scalar(stl[:, oc, :], tmpC[:, :],
                                                         lngc[:, oc:oc + 1], lnbc[:, oc:oc + 1],
                                                         OP.mult, OP.add))
                    vop(lambda e, oc=oc: e.tensor_tensor(tmpC[:, :], stl[:, oc, :], gt[:, oc, :], OP.mult))
                    vop(lambda e, oc=oc: e.tensor_tensor(stl[:, oc, :], tmpC[:, :], xt32c[:, oc, :], OP.add))
                vmark()
                pb.wait("sync", "eV")
                for kc in range(KC):
                    dma("sync", "dO", outD[kc, :, toks], stl[:, kc, :])

            if dbg:
                dma("gpsimd", "dS", xnD[:, :, :], xn_scr[:, :, :])
                dma("gpsimd", "dS", oD[:, :, :], o_scr[:, :, :])
                dma("gpsimd", "dS", gD[:, :, :], g_scr[:, :, :])
                dma("gpsimd", "dS", agD[:, :], agout[:, :])
                pb.wait("gpsimd", "dS")
            pb.wait("gpsimd", "dO")

        with nc.Block() as block:
            @block.tensor
            def _(e):
                pb.replay("tensor", e, semh)

            @block.vector
            def _(e):
                pb.replay("vector", e, semh)

            @block.scalar
            def _(e):
                pb.replay("scalar", e, semh)

            @block.gpsimd
            def _(e):
                pb.replay("gpsimd", e, semh)

            @block.sync
            def _(e):
                pb.replay("sync", e, semh)

    return nc


_NC_CACHE = {}


def kernel(**inputs):
    x = np.asarray(inputs["L0_attn_x"], dtype=np.float32)
    Wq = np.asarray(inputs["L0_attn_Wq"], dtype=np.float32)
    Wk = np.asarray(inputs["L0_attn_Wk"], dtype=np.float32)
    Wv = np.asarray(inputs["L0_attn_Wv"], dtype=np.float32)
    Wb = np.asarray(inputs["L0_attn_Wbeta"], dtype=np.float32)
    bb = np.asarray(inputs["L0_attn_b_beta"], dtype=np.float32)
    Wo = np.asarray(inputs["L0_attn_Wo"], dtype=np.float32)
    bo = np.asarray(inputs["L0_attn_b_o"], dtype=np.float32)
    lg = np.asarray(inputs["L0_attn_ln_g"], dtype=np.float32)
    lb = np.asarray(inputs["L0_attn_ln_b"], dtype=np.float32)
    W1 = np.asarray(inputs["L0_ffn_W1"], dtype=np.float32)
    b1 = np.asarray(inputs["L0_ffn_b1"], dtype=np.float32)
    W2 = np.asarray(inputs["L0_ffn_W2"], dtype=np.float32)
    b2 = np.asarray(inputs["L0_ffn_b2"], dtype=np.float32)
    sc = np.asarray(inputs["L0_ffn_scale"], dtype=np.float32)

    bsel = np.kron(np.eye(NH, dtype=np.float32), np.ones((D, 1), np.float32))
    common = {
        "wq_t": np.ascontiguousarray(Wq.T).reshape(KC, P, 1024),
        "wv_t": np.ascontiguousarray(Wv.T).reshape(KC, P, 1024),
        "wk": np.ascontiguousarray(Wk).reshape(KC, P, 1024),
        "wo": np.ascontiguousarray(Wo).reshape(KC, P, 1024),
        "w1_t": np.ascontiguousarray(W1.T).reshape(KC, P, 4096),
        "w2_t": np.ascontiguousarray(W2.T).reshape(HC, P, 1024),
        "wbeta": np.ascontiguousarray(Wb).reshape(KC, P, NH),
        "bsel": np.ascontiguousarray(bsel).reshape(KC, P, NH),
        "bselT": np.ascontiguousarray(bsel.T),
        "bcol_d": bb.reshape(NH, 1).copy(),
        "bocol_d": np.ascontiguousarray(bo.reshape(KC, P).T),
        "lngc_d": np.ascontiguousarray(lg.reshape(KC, P).T),
        "lnbc_d": np.ascontiguousarray(lb.reshape(KC, P).T),
        "b1c_d": np.ascontiguousarray(b1.reshape(HC, P).T),
        "b2c_d": np.ascontiguousarray(b2.reshape(KC, P).T),
        "sccol_d": np.full((P, 1), float(sc), np.float32),
        "idin": np.eye(P, dtype=np.float32),
    }
    in_maps = []
    for c in range(NCORE):
        b, q = c // PER_B, c % PER_B
        SEL = np.zeros((PER_B, PER_B), np.float32)
        for l in range(PER_B):
            for p in range(PER_B):
                if p < l < q:
                    SEL[l, p] = 1.0
        pmkv = np.array([[1.0 if p < q else 0.0] for p in range(PER_B)], np.float32)
        m = dict(common)
        m["xT"] = np.ascontiguousarray(
            x[b, q * TOK:(q + 1) * TOK, :].T).reshape(KC, P, TOK)
        m["xlora"] = np.ascontiguousarray(x[b, 1:17, :])
        m["selS_d"] = SEL
        m["pmk_d"] = pmkv
        in_maps.append(m)

    if "nc" not in _NC_CACHE:
        _NC_CACHE["nc"] = build()
    nc = _NC_CACHE["nc"]
    res = run_bass_kernel_spmd(nc, in_maps, core_ids=list(range(NCORE)))
    out = np.empty((2, 8192, 1024), np.float32)
    for c in range(NCORE):
        b, q = c // PER_B, c % PER_B
        o = np.asarray(res.results[c]["outD"]).reshape(1024, TOK)
        out[b, q * TOK:(q + 1) * TOK, :] = o.T
    return out


# version 08 (final)
# speedup vs baseline: 1.0267x; 1.0267x over previous
import numpy as np
import concourse.bass as bass
import concourse.mybir as mybir
from concourse.bass_utils import run_bass_kernel_spmd
from contextlib import ExitStack

F32 = mybir.dt.float32
BF16 = mybir.dt.bfloat16
AF = mybir.ActivationFunctionType
OP = mybir.AluOpType

P = 128
KC = 8
HC = 32
NH = 16
D = 64
TOK = 2048
TN = 512
NT = TOK // TN
HN = 256
EPS = 1e-5
NCORE = 8
PER_B = 4


class PB:
    def __init__(self):
        self.plan = []
        self.cnt = {}

    def emit(self, eng, fn, inc=None):
        self.plan.append((eng, fn, inc))
        if inc is not None:
            self.cnt[inc[0]] = self.cnt.get(inc[0], 0) + inc[1]

    def wait(self, eng, sem, val=None):
        v = self.cnt.get(sem, 0) if val is None else val
        if v <= 0:
            return
        self.plan.append((eng, ("w", sem, v), None))

    def snap(self, sem):
        return self.cnt.get(sem, 0)

    def replay(self, eng_name, e, semh):
        for eng, fn, inc in self.plan:
            if eng != eng_name:
                continue
            if isinstance(fn, tuple):
                e.wait_ge(semh[fn[1]], fn[2])
            else:
                inst = fn(e)
                if inc is not None and inst is not None:
                    inst.then_inc(semh[inc[0]], inc[1])


def build(dbg=False):
    nc = bass.Bass("TRN2", target_bir_lowering=False, detect_race_conditions=False)
    pb = PB()

    xTd = nc.dram_tensor("xT", [KC, P, TOK], F32, kind="ExternalInput")
    xlod = nc.dram_tensor("xlora", [NH, 1024], F32, kind="ExternalInput")
    wqtd = nc.dram_tensor("wq_t", [KC, P, 1024], F32, kind="ExternalInput")
    wvtd = nc.dram_tensor("wv_t", [KC, P, 1024], F32, kind="ExternalInput")
    wkd = nc.dram_tensor("wk", [KC, P, 1024], F32, kind="ExternalInput")
    wod = nc.dram_tensor("wo", [KC, P, 1024], F32, kind="ExternalInput")
    w1d = nc.dram_tensor("w1_t", [KC, P, 4096], F32, kind="ExternalInput")
    w2d = nc.dram_tensor("w2_t", [HC, P, 1024], F32, kind="ExternalInput")
    wbd = nc.dram_tensor("wbeta", [KC, P, NH], F32, kind="ExternalInput")
    bseld = nc.dram_tensor("bsel", [KC, P, NH], F32, kind="ExternalInput")
    bselTd = nc.dram_tensor("bselT", [NH, 1024], F32, kind="ExternalInput")
    bcold = nc.dram_tensor("bcol_d", [NH, 1], F32, kind="ExternalInput")
    bocold = nc.dram_tensor("bocol_d", [P, KC], F32, kind="ExternalInput")
    lngd = nc.dram_tensor("lngc_d", [P, KC], F32, kind="ExternalInput")
    lnbd = nc.dram_tensor("lnbc_d", [P, KC], F32, kind="ExternalInput")
    b1cd = nc.dram_tensor("b1c_d", [P, HC], F32, kind="ExternalInput")
    b2cd = nc.dram_tensor("b2c_d", [P, KC], F32, kind="ExternalInput")
    sccd = nc.dram_tensor("sccol_d", [P, 1], F32, kind="ExternalInput")
    idind = nc.dram_tensor("idin", [P, P], F32, kind="ExternalInput")
    seld = nc.dram_tensor("selS_d", [PER_B, PER_B], F32, kind="ExternalInput")
    pmkd = nc.dram_tensor("pmk_d", [PER_B, 1], F32, kind="ExternalInput")
    outD = nc.dram_tensor("outD", [KC, P, TOK], F32, kind="ExternalOutput")
    xn_scr = nc.dram_tensor("xn_scr", [P, KC, TOK], BF16)
    o_scr = nc.dram_tensor("o_scr", [P, KC, TOK], BF16)
    g_scr = nc.dram_tensor("g_scr", [P, KC, TOK], F32)
    w1_bscr = nc.dram_tensor("w1_bscr", [KC, P, 4096], BF16)
    w2_bscr = nc.dram_tensor("w2_bscr", [HC, P, 1024], BF16)
    agin = nc.dram_tensor("agin", [1, 1152], F32)
    agout = nc.dram_tensor("agout", [PER_B, 1152], F32)
    if dbg:
        xnD = nc.dram_tensor("xnD", [P, KC, TOK], BF16, kind="ExternalOutput")
        oD = nc.dram_tensor("oD", [P, KC, TOK], BF16, kind="ExternalOutput")
        gD = nc.dram_tensor("gD", [P, KC, TOK], F32, kind="ExternalOutput")
        agD = nc.dram_tensor("agD", [PER_B, 1152], F32, kind="ExternalOutput")
        hD = nc.dram_tensor("hD", [P, HC, HN], BF16, kind="ExternalOutput")
        tD = nc.dram_tensor("tD", [NH, HN], BF16, kind="ExternalOutput")
        t2wD = nc.dram_tensor("t2wD", [P, HC, NH], BF16, kind="ExternalOutput")
        e1zD = nc.dram_tensor("e1zD", [4, HC, P], BF16, kind="ExternalOutput")
        l2zD = nc.dram_tensor("l2zD", [NH, KC, P], BF16, kind="ExternalOutput")
        xnlD = nc.dram_tensor("xnlD", [P, KC, NH], BF16, kind="ExternalOutput")
        xlrD = nc.dram_tensor("xlrD", [NH, 1024], BF16, kind="ExternalOutput")
        colD = nc.dram_tensor("colD", [NH, 4], F32, kind="ExternalOutput")

    with ExitStack() as es0:
        A = lambda name, shp, dt: es0.enter_context(nc.sbuf_tensor(name, shp, dt))
        qvbw = A("qvbw", [P, KC, 100], BF16)
        xlr = A("xlr", [NH, 1024], BF16)
        xlrF = A("xlrF", [NH, 1024], F32)
        xnlT = A("xnlT", [P, KC, NH], BF16)
        e1z = A("e1z", [4, HC, P], BF16)
        l2z = A("l2z", [NH, KC, P], BF16)
        t2w = A("t2w", [P, HC, NH], BF16)
        bselB = A("bselB", [P, KC, NH], BF16)
        bselTB = A("bselTB", [NH, 1024], BF16)
        bcol = A("bcol", [NH, 1], F32)
        bocol = A("bocol", [P, KC], F32)
        lngc = A("lngc", [P, KC], F32)
        lnbc = A("lnbc", [P, KC], F32)
        b1c = A("b1c", [P, HC], F32)
        b2c = A("b2c", [P, KC], F32)
        sccol = A("sccol", [P, 1], F32)
        epsc = A("epsc", [P, 1], F32)
        idinS = A("idinS", [P, P], F32)
        selS = A("selS", [PER_B, PER_B], F32)
        pmk = A("pmk", [PER_B, 1], F32)
        onesb = A("onesb", [P, P], BF16)
        xloJ = A("xloJ", [NH, 1024], F32)
        colA = A("colA", [NH, 1], F32)
        colB = A("colB", [NH, 1], F32)
        colC = A("colC", [NH, 1], F32)
        mrk = A("mrk", [1, 4], F32)
        zc = A("zc", [NH, 8], F32)
        t1r = A("t1r", [4, TOK], BF16)
        q2b = A("q2b", [NH, TOK], BF16)
        hstate = A("hstate", [P, 9], F32)
        urow9 = A("urow9", [9, P], F32)
        agS = A("agS", [PER_B, 1152], F32)
        wmat = A("wmat", [PER_B, NH], F32)
        wmat2 = A("wmat2", [PER_B, NH], F32)
        wm2tb = A("wm2tb", [NH, PER_B], BF16)
        hgS = A("hgS", [P, KC], F32)
        hbd = A("hbd", [P, KC, NH], BF16)
        hbdlo = A("hbdlo", [P, KC, NH], BF16)
        whb = A("whb", [NH, 1024], BF16)
        whblo = A("whblo", [NH, 1024], BF16)
        q2lo = A("q2lo", [NH, TOK], BF16)
        wobf = A("wobf", [P, KC, 1024], BF16)
        xnb = A("xnb", [P, KC, TN], BF16)
        ob = A("ob", [P, KC, TN], BF16)
        gt = A("gt", [P, KC, TN], F32)
        muS = A("muS", [P, TN], F32)
        varS = A("varS", [P, TN], F32)
        rstdS = A("rstdS", [P, TN], F32)
        tmpA = A("tmpA", [P, TN], F32)
        tmpC = A("tmpC", [P, TN], F32)
        pbank = [es0.enter_context(nc.psum_tensor("pb%d" % i, [P, 512], F32))
                 for i in range(8)]
        psA, psB, psC, psD, psE, psF, psG, psm = pbank

        semh = {}
        for s in ("dI", "dS", "dO", "eT", "eV", "eS", "cc"):
            semh[s] = es0.enter_context(nc.semaphore(s))

        def dma(eng, sem, dst, src):
            pb.emit(eng, lambda e, d=dst, s=src: e.dma_start(out=d, in_=s), (sem, 16))

        def mm(out, lhsT, rhs, start, stop, inc=None):
            pb.emit("tensor",
                    lambda e, o=out, l=lhsT, r=rhs, st=start, sp=stop:
                    e.matmul(o, l, r, start=st, stop=sp), inc)

        def vop(fn, inc=None):
            pb.emit("vector", fn, inc)

        def sop(fn, inc=None):
            pb.emit("scalar", fn, inc)

        def gop(fn, inc=None):
            pb.emit("gpsimd", fn, inc)

        def vmark():
            vop(lambda e: e.memset(mrk[0:1, 0:4], 0.0), ("eV", 1))

        for dst, src in [(bcol[:], bcold[:, :]), (bocol[:], bocold[:, :]),
                         (lngc[:], lngd[:, :]), (lnbc[:], lnbd[:, :]),
                         (b1c[:], b1cd[:, :]), (b2c[:], b2cd[:, :]),
                         (sccol[:], sccd[:, :]), (idinS[:], idind[:, :]),
                         (selS[:], seld[:, :]), (pmk[:], pmkd[:, :]),
                         (xloJ[:], xlod[:, :])]:
            dma("sync", "dI", dst, src)

        gop(lambda e: e.memset(onesb[:, :], 1.0))
        gop(lambda e: e.memset(epsc[:, :], EPS))
        gop(lambda e: e.memset(hstate[:, :], 0.0))
        gop(lambda e: e.memset(qvbw[:, :, :], 0.0))
        gop(lambda e: e.memset(e1z[0:4, :, :], 0.0))
        gop(lambda e: e.memset(l2z[:, :, :], 0.0))
        gop(lambda e: e.memset(t2w[:, :, :], 0.0), ("cc", 1))
        pb.wait("gpsimd", "cc", 1)
        for kc in range(KC):
            dma("gpsimd", "dS", bselB[:, kc, :], bseld[kc, :, :])
            dma("gpsimd", "dS", qvbw[:, kc, 64:80], wbd[kc, :, :])
        dma("gpsimd", "dS", bselTB[:], bselTd[:, :])
        smallcast_mark = pb.snap("dS")

        with ExitStack() as esA:
            AA = lambda name, shp, dt: esA.enter_context(nc.sbuf_tensor(name, shp, dt))
            wkbf = AA("wkbf", [P, KC, 1024], BF16)
            xt32 = AA("xt32", [P, KC, TN], F32)
            xb = AA("xb", [P, KC, TN], BF16)
            x2b = AA("x2b", [P, KC, TN], BF16)
            vsbS = AA("vsbS", [P, TN], F32)
            d0s = AA("d0s", [P, TN], F32)
            d1s = AA("d1s", [P, TN], F32)
            hsc = AA("hsc", [P, TN], F32)
            betar = AA("betar", [NH, TOK], F32)
            qsr = AA("qsr", [NH, TOK], F32)
            cpb = AA("cpb", [NH, TOK], F32)
            betab16 = AA("betab16", [NH, TOK], BF16)
            betalo = AA("betalo", [NH, TOK], BF16)
            qsrb = AA("qsrb", [NH, TOK], BF16)
            vsrb = AA("vsrb", [NH, TOK], BF16)

            gop(lambda e: e.memset(zc[:, :], 0.0), ("cc", 1))

            pb.wait("vector", "dI")
            vop(lambda e: e.tensor_reduce(colA[:, 0:1], xloJ[:, :], mybir.AxisListType.X, OP.add), ("eV", 1))
            pb.wait("vector", "eV")
            vop(lambda e: e.tensor_scalar(colC[:, 0:1], colA[:, 0:1], 1.0 / 1024, None, OP.mult), ("eV", 1))
            pb.wait("vector", "eV")
            vop(lambda e: e.tensor_scalar(xlrF[:, :], xloJ[:, :], colC[:, 0:1], None, OP.subtract), ("eV", 1))
            pb.wait("scalar", "eV")
            sop(lambda e: e.activation(xloJ[:, :], xlrF[:, :], AF.Square, accum_out=colB[:, 0:1]))
            sop(lambda e: e.activation(colC[:, 0:1], colB[:, 0:1], AF.Copy), ("eS", 1))
            pb.wait("vector", "eS")
            vop(lambda e: e.tensor_scalar(colB[:, 0:1], colC[:, 0:1], 1.0 / 1024, None, OP.mult), ("eV", 1))
            pb.wait("scalar", "eV")
            sop(lambda e: e.activation(colA[:, 0:1], colB[:, 0:1], AF.Ln, bias=epsc[0:NH, 0:1]))
            sop(lambda e: e.activation(colB[:, 0:1], colA[:, 0:1], AF.Exp, scale=-0.5), ("eS", 1))
            pb.wait("vector", "eS")
            vop(lambda e: e.tensor_scalar(xloJ[:, :], xlrF[:, :], colB[:, 0:1], None, OP.mult))
            vop(lambda e: e.tensor_copy(xlr[:, :], xloJ[:, :]), ("eV", 1))

            pb.wait("tensor", "dS", smallcast_mark)
            pb.wait("tensor", "dI")
            for kc in range(KC):
                pb.wait("tensor", "eV")
                mm(psm[0:P, 0:NH], xloJ[0:NH, kc * P:(kc + 1) * P], idinS[0:NH, 0:NH],
                   True, True, inc=("eT", 1))
                pb.wait("vector", "eT")
                vop(lambda e, kc=kc: e.tensor_copy(xnlT[:, kc, :], psm[0:P, 0:NH]), ("eV", 1))

            pb.wait("vector", "cc", 2)
            for kc in range(KC):
                vop(lambda e, kc=kc: e.tensor_copy(qvbw[:, kc, 96:100], xnlT[:, kc, 1:9:2]))
            for hc in range(HC):
                m = hc // 8
                sub = hc % 8
                for i in range(4):
                    vop(lambda e, hc=hc, sub=sub, i=i, m=m: e.tensor_copy(
                        t2w[:, hc, 4 * i + m:4 * i + m + 1], xnlT[:, sub, 9 + 2 * i:10 + 2 * i]))
            vmark()
            pb.wait("gpsimd", "eV")
            for i in range(4):
                for sub in range(KC):
                    hc = 8 * i + sub
                    dma("gpsimd", "dS", e1z[i:i + 1, hc, :],
                        xlr[2 * i:2 * i + 1, sub * P:(sub + 1) * P])
            for oc in range(KC):
                i = oc // 2
                p0 = (oc % 2) * P
                for m in range(4):
                    st = 4 * p0 + m
                    pb.emit("gpsimd", lambda e, i=i, m=m, oc=oc, st=st: e.dma_start(
                        out=l2z[4 * i + m:4 * i + m + 1, oc, :],
                        in_=xlr[8 + 2 * i:9 + 2 * i, st:st + 4 * (P - 1) + 1:4],
                        allow_slow_non_contiguous=True), ("dS", 16))

            for kc in range(KC):
                dma("gpsimd", "dS", wkbf[:, kc, :], wqtd[kc, :, :])
            wq_mark = pb.snap("dS")

            def wqs_mm(dst_off, ready_mark):
                pb.wait("tensor", "dS", ready_mark)
                for dc in range(KC):
                    pb.wait("tensor", "eV")
                    for jc in range(KC):
                        mm(psm[0:P, 16:32], wkbf[:, jc, dc * P:(dc + 1) * P],
                           bselB[:, jc, :], jc == 0, jc == KC - 1,
                           inc=("eT", 1) if jc == KC - 1 else None)
                    pb.wait("vector", "eT")
                    vop(lambda e, dc=dc, o=dst_off: e.tensor_copy(
                        qvbw[:, dc, o:o + 16], psm[0:P, 16:32]), ("eV", 1))

            wqs_mm(0, wq_mark)
            pb.wait("gpsimd", "eV")
            for kc in range(KC):
                dma("gpsimd", "dS", wkbf[:, kc, :], wvtd[kc, :, :])
            wv_mark = pb.snap("dS")
            wqs_mm(32, wv_mark)
            pb.wait("gpsimd", "eV")
            for kc in range(KC):
                dma("gpsimd", "dS", wkbf[:, kc, :], wkd[kc, :, :])
            for kc in range(KC):
                dma("gpsimd", "dS", wobf[:, kc, :], wod[kc, :, :])
            wk_mark = pb.snap("dS")
            for jc in range(KC):
                dma("gpsimd", "dS", w1_bscr[jc, :, :], w1d[jc, :, :])
            w1cast_mark = pb.snap("dS")

            w2cast_marks = []
            for tt in range(NT):
                toks = slice(tt * TN, (tt + 1) * TN)
                pb.wait("sync", "eV")
                pb.wait("sync", "eS")
                for kc in range(KC):
                    dma("sync", "dI", xt32[:, kc, :], xTd[kc, :, toks])
                pb.wait("vector", "dI")
                pb.wait("scalar", "dI")
                pb.wait("scalar", "eT")
                for kc in range(KC):
                    vop(lambda e, kc=kc: e.tensor_copy(xb[:, kc, :], xt32[:, kc, :]),
                        ("eV", 1) if kc == KC - 1 else None)
                for kc in range(KC):
                    sop(lambda e, kc=kc: e.activation(x2b[:, kc, :], xt32[:, kc, :], AF.Square),
                        ("eS", 1) if kc == KC - 1 else None)
                pb.wait("tensor", "eV")
                pb.wait("tensor", "eS")
                for kc in range(KC):
                    mm(psC[:, :], onesb[:, :], xb[:, kc, :], kc == 0, kc == KC - 1,
                       inc=("eT", 1) if kc == KC - 1 else None)
                for kc in range(KC):
                    mm(psD[:, :], onesb[:, :], x2b[:, kc, :], kc == 0, kc == KC - 1,
                       inc=("eT", 1) if kc == KC - 1 else None)
                pb.wait("vector", "eT")
                vop(lambda e: e.tensor_scalar(muS[:, :], psC[:, :], 1.0 / 1024, None, OP.mult), ("eV", 1))
                pb.wait("scalar", "eV")
                sop(lambda e: e.activation(tmpA[:, :], muS[:, :], AF.Square), ("eS", 1))
                pb.wait("vector", "eS")
                vop(lambda e: e.scalar_tensor_tensor(varS[:, :], psD[:, :], 1.0 / 1024,
                                                     tmpA[:, :], OP.mult, OP.subtract), ("eV", 1))
                pb.wait("scalar", "eV")
                sop(lambda e: e.activation(tmpA[:, :], varS[:, :], AF.Ln, bias=epsc[:, 0:1]))
                sop(lambda e: e.activation(rstdS[:, :], tmpA[:, :], AF.Exp, scale=-0.5), ("eS", 1))
                pb.wait("vector", "eS")
                pb.wait("vector", "dS")
                for kc in range(KC):
                    vop(lambda e, kc=kc: e.tensor_tensor(tmpC[:, :], xt32[:, kc, :], muS[:, :], OP.subtract))
                    vop(lambda e, kc=kc: e.tensor_tensor(xnb[:, kc, :], tmpC[:, :], rstdS[:, :], OP.mult))
                vmark()
                pb.wait("gpsimd", "eV")
                dma("gpsimd", "dS", xn_scr[:, :, toks], xnb[:, :, :])

                pb.wait("tensor", "eV")
                pb.wait("tensor", "dS", wk_mark)
                for jc in range(KC):
                    mm(psm[0:100, :], qvbw[:, jc, :], xnb[:, jc, :], jc == 0, jc == KC - 1,
                       inc=("eT", 1) if jc == KC - 1 else None)
                pb.wait("scalar", "eT")
                pb.wait("scalar", "eV")
                sop(lambda e, toks=toks: e.activation(betar[:, toks], psm[64:80, :], AF.Sigmoid, bias=bcol[:, 0:1]))
                sop(lambda e, toks=toks: e.activation(betab16[:, toks], betar[:, toks], AF.Copy))
                sop(lambda e, toks=toks: e.activation(qsrb[:, toks], psm[0:16, :], AF.Copy))
                sop(lambda e, toks=toks: e.activation(qsr[:, toks], psm[0:16, :], AF.Copy))
                sop(lambda e, toks=toks: e.activation(vsrb[:, toks], psm[32:48, :], AF.Copy))
                sop(lambda e, toks=toks: e.activation(t1r[:, toks], psm[96:100, :], AF.Copy,
                                                      scale=sccol[0:4, 0:1]), ("eS", 1))
                pb.wait("vector", "eS")
                vop(lambda e, toks=toks: e.tensor_tensor(betalo[:, toks], betar[:, toks],
                                                         betab16[:, toks], OP.subtract), ("eV", 1))

                pb.wait("vector", "dS")
                for fc in range(KC):
                    psk = pbank[fc % 2]
                    pb.wait("tensor", "eV")
                    pb.wait("tensor", "eS")
                    for jc in range(KC):
                        mm(psk[:, :], wkbf[:, jc, fc * P:(fc + 1) * P], xnb[:, jc, :],
                           jc == 0, jc == KC - 1)
                    bsl = bselTB[0:NH, fc * P:(fc + 1) * P]
                    mm(psE[:, :], bsl, vsrb[:, toks], True, True)
                    mm(psF[:, :], bsl, qsrb[:, toks], True, True)
                    mm(psG[:, :], bsl, betab16[:, toks], True, False)
                    mm(psG[:, :], bsl, betalo[:, toks], False, True, inc=("eT", 1))
                    pb.wait("scalar", "eT")
                    pb.wait("scalar", "eV")
                    sop(lambda e: e.activation(vsbS[:, :], psE[:, :], AF.Copy))
                    sop(lambda e: e.activation(d0s[:, :], psG[:, :], AF.Copy), ("eS", 1))
                    pb.wait("vector", "eT")
                    pb.wait("vector", "eS")
                    vop(lambda e, psk=psk: e.tensor_tensor(d1s[:, :], psk[:, :], vsbS[:, :], OP.mult))
                    vop(lambda e, fc=fc: e.tensor_tensor_scan(
                        hsc[:, :], d0s[:, :], d1s[:, :], hstate[:, fc:fc + 1], OP.mult, OP.add), ("eV", 1))
                    pb.wait("vector", "eV")
                    vop(lambda e, fc=fc: e.tensor_copy(hstate[:, fc:fc + 1], hsc[:, TN - 1:TN]))
                    vop(lambda e, fc=fc: e.tensor_tensor(ob[:, fc, :], hsc[:, :], psF[:, :], OP.mult), ("eV", 1))
                pb.wait("gpsimd", "eV")
                dma("gpsimd", "dS", o_scr[:, :, toks], ob[:, :, :])
                for hcw in range(tt * 8, tt * 8 + 8):
                    dma("gpsimd", "dS", w2_bscr[hcw, :, :], w2d[hcw, :, :])
                w2cast_marks.append(pb.snap("dS"))

            sop(lambda e: e.activation(cpb[:, :], betar[:, :], AF.Ln, accum_out=colA[:, 0:1]))
            sop(lambda e: e.activation(colC[:, 0:1], colA[:, 0:1], AF.Copy), ("eS", 1))
            pb.wait("vector", "eS")
            vop(lambda e: e.tensor_copy(hstate[0:NH, 8:9], colC[:, 0:1]))
            vop(lambda e: e.tensor_tensor_scan(cpb[:, :], betar[:, :],
                                               zc[:, 0:1].broadcast_to([NH, TOK]),
                                               1.0, OP.mult, OP.add), ("eV", 1))
            pb.wait("vector", "eV")
            vop(lambda e: e.tensor_tensor(cpb[:, :], qsr[:, :], cpb[:, :], OP.mult), ("eV", 1))
            pb.wait("vector", "eV")
            vop(lambda e: e.tensor_copy(q2b[:, :], cpb[:, :]), ("eV", 1))
            pb.wait("vector", "eV")
            vop(lambda e: e.tensor_tensor(q2lo[:, :], cpb[:, :], q2b[:, :], OP.subtract), ("eV", 1))
            pb.wait("tensor", "eV")
            mm(psm[0:9, 0:P], hstate[:, 0:9], idinS[:, :], True, True, inc=("eT", 1))
            pb.wait("vector", "eT")
            vop(lambda e: e.tensor_copy(urow9[:, :], psm[0:9, 0:P]), ("eV", 1))
            pb.wait("gpsimd", "eV")
            dma("gpsimd", "dS", bass.AP(agin, 0, [[P, 9], [1, P]]), urow9[0:9, :])
            pb.wait("gpsimd", "dS")
            rg = [[0, 1, 2, 3], [4, 5, 6, 7]]
            pb.emit("gpsimd", lambda e: e.collective_compute(
                "AllGather", OP.bypass, replica_groups=rg,
                ins=[agin.ap().opt()], outs=[agout.ap().opt()]), ("cc", 1))
            pb.wait("gpsimd", "cc")
            dma("gpsimd", "dS", agS[:, :], agout[:, :])
            pb.wait("tensor", "dS")
            mm(psm[0:PER_B, 0:NH], selS[:, :], agS[0:PER_B, 1024:1040], True, True, inc=("eT", 1))
            pb.wait("scalar", "eT")
            sop(lambda e: e.activation(wmat[:, :], psm[0:PER_B, 0:NH], AF.Exp), ("eS", 1))
            pb.wait("vector", "eS")
            vop(lambda e: e.tensor_scalar(wmat2[:, :], wmat[:, :], pmk[:, 0:1], None, OP.mult), ("eV", 1))
            pb.wait("vector", "eV")
            pb.wait("tensor", "eV")
            mm(psm[0:NH, 32:32 + PER_B], wmat2[0:PER_B, 0:NH], idinS[0:PER_B, 0:PER_B],
               True, True, inc=("eT", 1))
            pb.wait("vector", "eT")
            vop(lambda e: e.tensor_copy(wm2tb[:, :], psm[0:NH, 32:32 + PER_B]), ("eV", 1))
            for fc in range(KC):
                pb.wait("tensor", "eV")
                mm(psm[0:P, 24:24 + PER_B], agS[0:PER_B, fc * P:(fc + 1) * P],
                   idinS[0:PER_B, 0:PER_B], True, True, inc=("eT", 1))
                mm(psD[:, 0:PER_B], bselTB[0:NH, fc * P:(fc + 1) * P], wm2tb[0:NH, 0:PER_B],
                   True, True, inc=("eT", 1))
                pb.wait("vector", "eT")
                vop(lambda e: e.tensor_copy(muS[:, 0:PER_B], psm[0:P, 24:24 + PER_B]), ("eV", 1))
                pb.wait("vector", "eV")
                vop(lambda e: e.tensor_tensor(varS[:, 0:PER_B], muS[:, 0:PER_B], psD[:, 0:PER_B], OP.mult), ("eV", 1))
                pb.wait("vector", "eV")
                vop(lambda e, fc=fc: e.tensor_reduce(hgS[:, fc:fc + 1], varS[:, 0:PER_B],
                                                     mybir.AxisListType.X, OP.add), ("eV", 1))
                pb.wait("vector", "eV")
            pb.wait("vector", "eV")
            for fc in range(KC):
                vop(lambda e, fc=fc: e.tensor_scalar(
                    varS[:, 0:NH], bselB[:, fc, :], hgS[:, fc:fc + 1], None, OP.mult), ("eV", 1))
                pb.wait("vector", "eV")
                vop(lambda e, fc=fc: e.tensor_copy(hbd[:, fc, :], varS[:, 0:NH]), ("eV", 1))
                pb.wait("vector", "eV")
                vop(lambda e, fc=fc: e.tensor_tensor(hbdlo[:, fc, :], varS[:, 0:NH],
                                                     hbd[:, fc, :], OP.subtract))
            vmark()
            for half in range(2):
                pb.wait("tensor", "eV")
                for fc in range(KC):
                    mm(psA[0:NH, :], hbd[:, fc, :], wobf[:, fc, half * 512:(half + 1) * 512],
                       fc == 0, False)
                for fc in range(KC):
                    mm(psA[0:NH, :], hbdlo[:, fc, :], wobf[:, fc, half * 512:(half + 1) * 512],
                       False, fc == KC - 1, inc=("eT", 1) if fc == KC - 1 else None)
                pb.wait("vector", "eT")
                vop(lambda e, half=half: e.tensor_copy(
                    whb[:, half * 512:(half + 1) * 512], psA[0:NH, :]), ("eV", 1))
                pb.wait("vector", "eV")
                vop(lambda e, half=half: e.tensor_tensor(
                    whblo[:, half * 512:(half + 1) * 512], psA[0:NH, :],
                    whb[:, half * 512:(half + 1) * 512], OP.subtract), ("eV", 1))

        with ExitStack() as esB:
            AB = lambda name, shp, dt: esB.enter_context(nc.sbuf_tensor(name, shp, dt))
            w1bf = AB("w1bf", [P, KC, 4096], BF16)
            hbt = AB("hbt", [P, HC, HN], BF16)
            w2s = AB("w2s", [P, KC, 1024], BF16)
            t2b = AB("t2b", [NH, HN], BF16)

            pb.wait("sync", "dS", w1cast_mark)
            pb.wait("sync", "eV")
            pb.wait("sync", "eS")
            pb.wait("sync", "eT")
            for jc in range(KC):
                dma("sync", "dI", w1bf[:, jc, :], w1_bscr[jc, :, :])
            w1_mark = pb.snap("dI")

            prev_w2_marks = {}
            for tt in range(NT):
                toks = slice(tt * TN, (tt + 1) * TN)
                pb.wait("sync", "eT")
                for kc in range(KC):
                    dma("sync", "dI", xnb[:, kc, :], xn_scr[:, kc, toks])
                for hf in range(2):
                    hts = slice(tt * TN + hf * HN, tt * TN + (hf + 1) * HN)
                    lts = slice(hf * HN, (hf + 1) * HN)
                    pb.wait("tensor", "dI")
                    pb.wait("tensor", "eS")
                    if tt == 0 and hf == 0:
                        pb.wait("tensor", "dI", w1_mark)
                    gelu_marks = {}
                    for hc in range(HC):
                        ph = pbank[hc % 2]
                        if hc >= 2:
                            pb.wait("tensor", "eS", gelu_marks[hc - 2])
                        for jc in range(KC):
                            mm(ph[:, 0:HN], w1bf[:, jc, hc * P:(hc + 1) * P],
                               xnb[:, jc, lts], jc == 0, False)
                        mm(ph[:, 0:HN], e1z[0:4, hc, :], t1r[0:4, hts], False, True,
                           inc=("eT", 1))
                        pb.wait("scalar", "eT")
                        sop(lambda e, hc=hc, ph=ph, lts=lts: e.activation(
                            hbt[:, hc, 0:HN], ph[:, 0:HN], AF.Gelu, bias=b1c[:, hc:hc + 1]),
                            ("eS", 1))
                        gelu_marks[hc] = pb.snap("eS")
                        pb.wait("tensor", "eS", gelu_marks[hc])
                        mm(psC[0:NH, 0:HN], t2w[:, hc, :], hbt[:, hc, 0:HN],
                           hc == 0, hc == HC - 1,
                           inc=("eT", 1) if hc == HC - 1 else None)
                    pb.wait("vector", "eT")
                    vop(lambda e: e.tensor_scalar(t2b[:, :], psC[0:NH, 0:HN],
                                                  sccol[0:NH, 0:1], None, OP.mult), ("eV", 1))
                    pb.wait("tensor", "eV")
                    pb.wait("tensor", "eS")
                    grp_marks = {}
                    for grp in range(KC):
                        if grp < 2:
                            if (grp + 6) in prev_w2_marks:
                                pb.wait("sync", "eT", prev_w2_marks[grp + 6])
                            else:
                                pb.wait("sync", "eT")
                        else:
                            pb.wait("sync", "eT", grp_marks[grp - 2])
                        if tt == 0 and hf == 0:
                            pb.wait("sync", "dS", w2cast_marks[min(grp // 2, NT - 1)])
                        for j in range(4):
                            hcr = grp * 4 + j
                            slot = hcr % KC
                            dma("sync", "dI", w2s[:, slot, :], w2_bscr[hcr, :, :])
                        pb.wait("tensor", "dI")
                        for j in range(4):
                            hcr = grp * 4 + j
                            slot = hcr % KC
                            for oc in range(KC):
                                mm(pbank[oc][:, 0:HN], w2s[:, slot, oc * P:(oc + 1) * P],
                                   hbt[:, hcr, 0:HN], hcr == 0, False,
                                   inc=("eT", 1) if (j == 3 and oc == KC - 1) else None)
                        grp_marks[grp] = pb.snap("eT")
                    prev_w2_marks = grp_marks
                    pb.wait("tensor", "eV")
                    for oc in range(KC):
                        mm(pbank[oc][:, 0:HN], l2z[:, oc, :], t2b[0:NH, :], False, True,
                           inc=("eT", 1) if oc == KC - 1 else None)
                    pb.wait("scalar", "eT")
                    pb.wait("scalar", "dS")
                    for oc in range(KC):
                        sop(lambda e, oc=oc, lts=lts: e.activation(
                            gt[:, oc, lts], pbank[oc][:, 0:HN], AF.Silu, bias=b2c[:, oc:oc + 1]),
                            ("eS", 1) if oc == KC - 1 else None)
                pb.wait("gpsimd", "eS")
                if dbg and tt == NT - 1:
                    dma("gpsimd", "dS", hD[:, :, :], hbt[:, :, :])
                    dma("gpsimd", "dS", tD[:, :], t2b[:, :])
                    dma("gpsimd", "dS", t2wD[:, :, :], t2w[:, :, :])
                    dma("gpsimd", "dS", e1zD[0:4, :, :], e1z[0:4, :, :])
                    dma("gpsimd", "dS", l2zD[:, :, :], l2z[:, :, :])
                    dma("gpsimd", "dS", xnlD[:, :, :], xnlT[:, :, :])
                    dma("gpsimd", "dS", xlrD[:, :], xlr[:, :])
                    pb.emit("gpsimd", lambda e: e.dma_start(
                        out=bass.AP(colD, 0, [[4, NH], [1, 1]]), in_=colB[:, 0:1],
                        allow_slow_non_contiguous=True), ("dS", 16))
                    pb.emit("gpsimd", lambda e: e.dma_start(
                        out=bass.AP(colD, 1, [[4, NH], [1, 1]]), in_=colC[:, 0:1],
                        allow_slow_non_contiguous=True), ("dS", 16))
                    pb.emit("gpsimd", lambda e: e.dma_start(
                        out=bass.AP(colD, 2, [[4, NH], [1, 1]]), in_=colA[:, 0:1],
                        allow_slow_non_contiguous=True), ("dS", 16))
                dma("gpsimd", "dS", g_scr[:, :, toks], gt[:, :, :])

        with ExitStack() as esC:
            AC = lambda name, shp, dt: esC.enter_context(nc.sbuf_tensor(name, shp, dt))
            xt32c = AC("xt32c", [P, KC, TN], F32)
            yt = AC("yt", [P, KC, TN], F32)
            ybb = AC("ybb", [P, KC, TN], BF16)
            y2b = AC("y2b", [P, KC, TN], BF16)
            stl = AC("stl", [P, KC, TN], F32)

            for tt in range(NT):
                toks = slice(tt * TN, (tt + 1) * TN)
                pb.wait("gpsimd", "eT")
                pb.wait("gpsimd", "eV")
                pb.wait("gpsimd", "eS")
                dma("gpsimd", "dS", ob[:, :, :], o_scr[:, :, toks])
                dma("gpsimd", "dS", xnb[:, :, :], xn_scr[:, :, toks])
                dma("gpsimd", "dS", gt[:, :, :], g_scr[:, :, toks])
                pb.wait("sync", "eV")
                pb.wait("sync", "eS")
                pb.wait("sync", "eT")
                for kc in range(KC):
                    dma("sync", "dI", xt32c[:, kc, :], xTd[kc, :, toks])
                for oc in range(KC):
                    pa = pbank[oc % 2]
                    pb.wait("tensor", "eV")
                    pb.wait("tensor", "eS")
                    pb.wait("tensor", "dS")
                    for fc in range(KC):
                        mm(pa[:, :], wobf[:, fc, oc * P:(oc + 1) * P], ob[:, fc, :],
                           fc == 0, False)
                    mm(pa[:, :], whb[0:NH, oc * P:(oc + 1) * P], q2b[0:NH, toks],
                       False, False)
                    mm(pa[:, :], whblo[0:NH, oc * P:(oc + 1) * P], q2b[0:NH, toks],
                       False, False)
                    mm(pa[:, :], whb[0:NH, oc * P:(oc + 1) * P], q2lo[0:NH, toks],
                       False, True, inc=("eT", 1))
                    pb.wait("vector", "eT")
                    pb.wait("vector", "dS")
                    vop(lambda e, oc=oc, pa=pa: e.scalar_tensor_tensor(
                        yt[:, oc, :], pa[:, :], bocol[:, oc:oc + 1], xnb[:, oc, :],
                        OP.add, OP.add), ("eV", 1))
                pb.wait("vector", "eV")
                for oc in range(KC):
                    vop(lambda e, oc=oc: e.tensor_copy(ybb[:, oc, :], yt[:, oc, :]),
                        ("eV", 1) if oc == KC - 1 else None)
                pb.wait("scalar", "eV")
                for oc in range(KC):
                    sop(lambda e, oc=oc: e.activation(y2b[:, oc, :], yt[:, oc, :], AF.Square),
                        ("eS", 1) if oc == KC - 1 else None)
                pb.wait("tensor", "eV")
                pb.wait("tensor", "eS")
                for oc in range(KC):
                    mm(psC[:, :], onesb[:, :], ybb[:, oc, :], oc == 0, oc == KC - 1,
                       inc=("eT", 1) if oc == KC - 1 else None)
                for oc in range(KC):
                    mm(psD[:, :], onesb[:, :], y2b[:, oc, :], oc == 0, oc == KC - 1,
                       inc=("eT", 1) if oc == KC - 1 else None)
                pb.wait("vector", "eT")
                vop(lambda e: e.tensor_scalar(muS[:, :], psC[:, :], 1.0 / 1024, None, OP.mult), ("eV", 1))
                pb.wait("scalar", "eV")
                sop(lambda e: e.activation(tmpA[:, :], muS[:, :], AF.Square), ("eS", 1))
                pb.wait("vector", "eS")
                vop(lambda e: e.scalar_tensor_tensor(varS[:, :], psD[:, :], 1.0 / 1024,
                                                     tmpA[:, :], OP.mult, OP.subtract), ("eV", 1))
                pb.wait("scalar", "eV")
                sop(lambda e: e.activation(tmpA[:, :], varS[:, :], AF.Ln, bias=epsc[:, 0:1]))
                sop(lambda e: e.activation(rstdS[:, :], tmpA[:, :], AF.Exp, scale=-0.5), ("eS", 1))
                pb.wait("vector", "eS")
                pb.wait("vector", "dI")
                pb.wait("vector", "dO")
                for oc in range(KC):
                    vop(lambda e, oc=oc: e.tensor_tensor(tmpC[:, :], yt[:, oc, :], muS[:, :], OP.subtract))
                    vop(lambda e, oc=oc: e.tensor_tensor(tmpC[:, :], tmpC[:, :], rstdS[:, :], OP.mult), ("eV", 1))
                    pb.wait("vector", "eV")
                    vop(lambda e, oc=oc: e.tensor_scalar(stl[:, oc, :], tmpC[:, :],
                                                         lngc[:, oc:oc + 1], lnbc[:, oc:oc + 1],
                                                         OP.mult, OP.add))
                    vop(lambda e, oc=oc: e.tensor_tensor(tmpC[:, :], stl[:, oc, :], gt[:, oc, :], OP.mult))
                    vop(lambda e, oc=oc: e.tensor_tensor(stl[:, oc, :], tmpC[:, :], xt32c[:, oc, :], OP.add))
                vmark()
                pb.wait("sync", "eV")
                for kc in range(KC):
                    dma("sync", "dO", outD[kc, :, toks], stl[:, kc, :])

            if dbg:
                dma("gpsimd", "dS", xnD[:, :, :], xn_scr[:, :, :])
                dma("gpsimd", "dS", oD[:, :, :], o_scr[:, :, :])
                dma("gpsimd", "dS", gD[:, :, :], g_scr[:, :, :])
                dma("gpsimd", "dS", agD[:, :], agout[:, :])
                pb.wait("gpsimd", "dS")
            pb.wait("gpsimd", "dO")

        with nc.Block() as block:
            @block.tensor
            def _(e):
                pb.replay("tensor", e, semh)

            @block.vector
            def _(e):
                pb.replay("vector", e, semh)

            @block.scalar
            def _(e):
                pb.replay("scalar", e, semh)

            @block.gpsimd
            def _(e):
                pb.replay("gpsimd", e, semh)

            @block.sync
            def _(e):
                pb.replay("sync", e, semh)

    return nc


_NC_CACHE = {}


def kernel(**inputs):
    x = np.asarray(inputs["L0_attn_x"], dtype=np.float32)
    Wq = np.asarray(inputs["L0_attn_Wq"], dtype=np.float32)
    Wk = np.asarray(inputs["L0_attn_Wk"], dtype=np.float32)
    Wv = np.asarray(inputs["L0_attn_Wv"], dtype=np.float32)
    Wb = np.asarray(inputs["L0_attn_Wbeta"], dtype=np.float32)
    bb = np.asarray(inputs["L0_attn_b_beta"], dtype=np.float32)
    Wo = np.asarray(inputs["L0_attn_Wo"], dtype=np.float32)
    bo = np.asarray(inputs["L0_attn_b_o"], dtype=np.float32)
    lg = np.asarray(inputs["L0_attn_ln_g"], dtype=np.float32)
    lb = np.asarray(inputs["L0_attn_ln_b"], dtype=np.float32)
    W1 = np.asarray(inputs["L0_ffn_W1"], dtype=np.float32)
    b1 = np.asarray(inputs["L0_ffn_b1"], dtype=np.float32)
    W2 = np.asarray(inputs["L0_ffn_W2"], dtype=np.float32)
    b2 = np.asarray(inputs["L0_ffn_b2"], dtype=np.float32)
    sc = np.asarray(inputs["L0_ffn_scale"], dtype=np.float32)

    bsel = np.kron(np.eye(NH, dtype=np.float32), np.ones((D, 1), np.float32))
    common = {
        "wq_t": np.ascontiguousarray(Wq.T).reshape(KC, P, 1024),
        "wv_t": np.ascontiguousarray(Wv.T).reshape(KC, P, 1024),
        "wk": np.ascontiguousarray(Wk).reshape(KC, P, 1024),
        "wo": np.ascontiguousarray(Wo).reshape(KC, P, 1024),
        "w1_t": np.ascontiguousarray(W1.T).reshape(KC, P, 4096),
        "w2_t": np.ascontiguousarray(W2.T).reshape(HC, P, 1024),
        "wbeta": np.ascontiguousarray(Wb).reshape(KC, P, NH),
        "bsel": np.ascontiguousarray(bsel).reshape(KC, P, NH),
        "bselT": np.ascontiguousarray(bsel.T),
        "bcol_d": bb.reshape(NH, 1).copy(),
        "bocol_d": np.ascontiguousarray(bo.reshape(KC, P).T),
        "lngc_d": np.ascontiguousarray(lg.reshape(KC, P).T),
        "lnbc_d": np.ascontiguousarray(lb.reshape(KC, P).T),
        "b1c_d": np.ascontiguousarray(b1.reshape(HC, P).T),
        "b2c_d": np.ascontiguousarray(b2.reshape(KC, P).T),
        "sccol_d": np.full((P, 1), float(sc), np.float32),
        "idin": np.eye(P, dtype=np.float32),
    }
    in_maps = []
    for c in range(NCORE):
        b, q = c // PER_B, c % PER_B
        SEL = np.zeros((PER_B, PER_B), np.float32)
        for l in range(PER_B):
            for p in range(PER_B):
                if p < l < q:
                    SEL[l, p] = 1.0
        pmkv = np.array([[1.0 if p < q else 0.0] for p in range(PER_B)], np.float32)
        m = dict(common)
        m["xT"] = np.ascontiguousarray(
            x[b, q * TOK:(q + 1) * TOK, :].T).reshape(KC, P, TOK)
        m["xlora"] = np.ascontiguousarray(x[b, 1:17, :])
        m["selS_d"] = SEL
        m["pmk_d"] = pmkv
        in_maps.append(m)

    if "nc" not in _NC_CACHE:
        _NC_CACHE["nc"] = build()
    nc = _NC_CACHE["nc"]
    res = run_bass_kernel_spmd(nc, in_maps, core_ids=list(range(NCORE)))
    out = np.empty((2, 8192, 1024), np.float32)
    for c in range(NCORE):
        b, q = c // PER_B, c % PER_B
        o = np.asarray(res.results[c]["outD"]).reshape(1024, TOK)
        out[b, q * TOK:(q + 1) * TOK, :] = o.T
    return out
